# version 34
# baseline (speedup 1.0000x reference)
import numpy as np
import ml_dtypes
from contextlib import ExitStack
from functools import partial

import concourse.bass as bass
import concourse.mybir as mybir
from concourse.bass_utils import run_bass_kernel_spmd

F32 = mybir.dt.float32
BF16 = mybir.dt.bfloat16
AF = mybir.ActivationFunctionType
ALU = mybir.AluOpType
AX = mybir.AxisListType

B = 8
D = 2048
T = 2048
SEG = 1024
ND = 16
EPS = 1e-5

ENGINES = ("tensor", "vector", "scalar", "sync", "gpsimd")


class Prog:

    def __init__(self, nc):
        self.nc = nc
        self.stream = {e: [] for e in ENGINES}
        self.cnt = {e: 0 for e in ENGINES}
        self.sems = {}
        for e in ENGINES:
            self.sems[("e", e)] = nc.alloc_semaphore(f"s_{e}")
        self.dchain = {}
        self.bufs = {}
        self.hw = {e: {} for e in ENGINES}

    def chain(self, name):
        if name not in self.dchain:
            self.sems[("d", name)] = self.nc.alloc_semaphore(f"d_{name}")
            self.dchain[name] = 0

    def emit(self, eng, fns, reads=(), writes=(), chain=None, accum=False):
        if callable(fns):
            fns = [fns]
        agg = {}

        def need(ev):
            if ev is None:
                return
            kind, nm, val = ev
            key = (kind, nm)
            if agg.get(key, 0) < val:
                agg[key] = val

        for k in reads:
            b = self.bufs.get(k)
            if b:
                need(b[0])
        for k in writes:
            b = self.bufs.get(k)
            if b:
                if not accum:
                    need(b[0])
                for ev in b[1]:
                    need(ev)
        for (kind, nm), val in sorted(agg.items(), key=lambda x: (x[0][0], str(x[0][1]), x[1])):
            cur = self.hw[eng].get((kind, nm), 0)
            if val > cur:
                self.stream[eng].append(("w", (kind, nm), val))
                self.hw[eng][(kind, nm)] = val
        if chain is not None:
            self.chain(chain)
            self.dchain[chain] += len(fns)
            ev = ("d", chain, 16 * self.dchain[chain])
            self.stream[eng].append(("i", fns, ("d", chain), 16, True))
        else:
            self.cnt[eng] += 1
            ev = ("e", eng, self.cnt[eng])
            self.stream[eng].append(("i", fns, ("e", eng), 1, False))
        for k in writes:
            self.bufs[k] = [ev, []]
        for k in reads:
            b = self.bufs.setdefault(k, [None, []])
            b[1].append(ev)
        return ev

    def raw_wait(self, eng, semkey, val):
        cur = self.hw[eng].get(semkey, 0)
        if val > cur:
            self.stream[eng].append(("w", semkey, val))
            self.hw[eng][semkey] = val

    def replay(self):
        with self.nc.Block() as block:
            for eng in ENGINES:
                items = self.stream[eng]
                if not items:
                    continue

                def body(h, items=items):
                    for it in items:
                        if it[0] == "w":
                            _, key, val = it
                            h.wait_ge(self.sems[key], val)
                        else:
                            _, fns, semkey, inc, inc_each = it
                            if inc_each:
                                for f in fns:
                                    f(h).then_inc(self.sems[semkey], inc)
                            else:
                                last = None
                                for f in fns:
                                    last = f(h)
                                last.then_inc(self.sems[semkey], inc)

                getattr(block, eng)(body)


def drive(tasks):
    live = [[gen, w] for gen, w in tasks]
    while live:
        for t in list(live):
            gen, w = t
            try:
                for _ in range(w):
                    next(gen)
            except StopIteration:
                live.remove(t)


def build_nc(units=(0, 1)):
    nc = bass.Bass("TRN2", target_bir_lowering=False)

    xdt = nc.dram_tensor("x_dt", [D, T], F32, kind="ExternalInput").ap()
    wq = nc.dram_tensor("wq", [16, 128, 2048], F32, kind="ExternalInput").ap()
    wk = nc.dram_tensor("wk", [4, 128, 2048], F32, kind="ExternalInput").ap()
    wv = nc.dram_tensor("wv", [4, 128, 2048], F32, kind="ExternalInput").ap()
    wo = nc.dram_tensor("wo", [16, 128, 2048], F32, kind="ExternalInput").ap()
    wb = nc.dram_tensor("wb", [16, 128, 2048], F32, kind="ExternalInput").ap()
    wr = nc.dram_tensor("wr", [128, 128], F32, kind="ExternalInput").ap()
    wat = nc.dram_tensor("wat", [128, 2048], F32, kind="ExternalInput").ap()
    wbct = nc.dram_tensor("wbct", [128, 2048], F32, kind="ExternalInput").ap()
    bbh = nc.dram_tensor("bb", [16, 128], F32, kind="ExternalInput").ap()
    e16h = nc.dram_tensor("e16", [8, 128], F32, kind="ExternalInput").ap()
    identh = nc.dram_tensor("ident", [128, 128], F32, kind="ExternalInput").ap()
    masksh = nc.dram_tensor("masks", [128, 2048], F32, kind="ExternalInput").ap()
    outT = nc.dram_tensor("out_T", [D, T], F32, kind="ExternalOutput").ap()

    ctx = ExitStack()
    e = ctx.enter_context

    xnb = e(nc.sbuf_tensor("xnb", [128, ND * SEG], BF16))
    sqb = e(nc.sbuf_tensor("sqb", [128, 2 * SEG], BF16))
    Q = e(nc.sbuf_tensor("Q", [128, 16 * SEG], BF16))
    K = e(nc.sbuf_tensor("K", [128, 4 * SEG], BF16))
    VT = e(nc.sbuf_tensor("VT", [128, 8 * 768], BF16))
    gb = e(nc.sbuf_tensor("gb", [128, 16 * SEG], BF16))
    Eb = e(nc.sbuf_tensor("Eb", [128, 2 * 2048], BF16))
    hb = e(nc.sbuf_tensor("hb", [128, SEG], BF16))
    hmb = e(nc.sbuf_tensor("hmb", [128, SEG], BF16))
    mEb = e(nc.sbuf_tensor("mEb", [128, SEG], BF16))
    mQf = e(nc.sbuf_tensor("mQf", [128, 64], F32))
    mxf = e(nc.sbuf_tensor("mxf", [128, 16], F32))
    mTb = e(nc.sbuf_tensor("mTb", [8, 1024], BF16))
    denb = e(nc.sbuf_tensor("denb", [128, 2 * 512], BF16))
    pbds = e(nc.sbuf_tensor("pbds", [128, 2 * 512], F32))
    pbr = e(nc.sbuf_tensor("pbr", [128, 2 * 512], F32))
    stA = e(nc.sbuf_tensor("stA", [128, 512], F32))
    stB = e(nc.sbuf_tensor("stB", [128, 512], F32))
    stRS = e(nc.sbuf_tensor("stRS", [128, 512], BF16))
    stMRS = e(nc.sbuf_tensor("stMRS", [128, 512], BF16))
    rsb = e(nc.sbuf_tensor("rsb", [128, SEG], BF16))
    mrsb = e(nc.sbuf_tensor("mrsb", [128, SEG], BF16))
    ws = e(nc.sbuf_tensor("ws", [128, 4 * 2048], BF16))
    wvs = e(nc.sbuf_tensor("wvs", [128, 2 * 2048], BF16))
    wrb = e(nc.sbuf_tensor("wrb", [128, 128], BF16))
    watb = e(nc.sbuf_tensor("watb", [128, 2048], BF16))
    wbctb = e(nc.sbuf_tensor("wbctb", [128, 2048], BF16))
    e16b = e(nc.sbuf_tensor("e16b", [8, 128], BF16))
    identf = e(nc.sbuf_tensor("identf", [128, 128], F32))
    masksb = e(nc.sbuf_tensor("masksb", [128, 2048], BF16))
    bbf = e(nc.sbuf_tensor("bbf", [16, 128], F32))
    bcol = e(nc.sbuf_tensor("bcol", [128, 16], F32))
    onesr = e(nc.sbuf_tensor("onesr", [128, 128], BF16))
    epsb = e(nc.sbuf_tensor("epsb", [128, 1], F32))
    oovr = e(nc.sbuf_tensor("oovr", [128, 1], BF16))
    xre = e(nc.sbuf_tensor("xre", [128, 2 * 512], F32))
    stg = e(nc.sbuf_tensor("stg", [128, 2 * 512], F32))
    sgf = e(nc.sbuf_tensor("sgf", [128, 2 * 512], F32))
    ps = e(nc.psum_tensor("ps", [128, 4096], F32))

    p = Prog(nc)

    def bank(b0, p0=0, p1=128, c0=0, w=512):
        return ps[p0:p1, 512 * b0 + c0:512 * b0 + c0 + w]

    PS = lambda b0: ("ps", b0)

    def AOsl(qc, r0, r1, blk):
        base = 8192 * qc + 512 * blk
        return xnb[r0:r1, base:base + 512]

    def ao_key(qc, blk, half):
        return ("AO", qc, blk, half)

    ALL_AO = [ao_key(qc, blk, h) for qc in range(2) for blk in range(16) for h in range(2)]

    p.emit("gpsimd", [
        lambda g: g.dma_start(out=wrb[:], in_=wr),
        lambda g: g.dma_start(out=watb[:], in_=wat),
        lambda g: g.dma_start(out=wbctb[:], in_=wbct),
        lambda g: g.dma_start(out=e16b[:], in_=e16h),
        lambda g: g.dma_start(out=masksb[:], in_=masksh),
    ], writes=[("wrb",), ("watb",), ("wbctb",), ("e16b",), ("masks",)], chain="cst")
    p.emit("sync", [
        lambda s: s.dma_start(out=identf[:], in_=identh),
        lambda s: s.dma_start(out=bbf[:], in_=bbh),
    ], writes=[("identf",), ("bbf",)], chain="sct")
    p.emit("vector", lambda v: v.memset(onesr[:], 1.0), writes=[("onesr",)])
    p.emit("vector", lambda v: v.memset(epsb[:], EPS), writes=[("epsb",)])
    p.emit("vector", lambda v: v.memset(oovr[:], 1.0 / 2048.0), writes=[("oovr",)])
    vt3 = VT[:].rearrange("q (t p c) -> q t p c", p=4, c=192)
    p.emit("vector", lambda v: v.memset(vt3[:, :, :, 64:65], 1.0), writes=[("vtc",)])
    p.emit("vector", lambda v: v.memset(vt3[:, :, :, 65:128], 0.0), writes=[("vtc",)])
    p.emit("tensor", lambda t: t.transpose(ps[0:128, 3584:3584 + 16], bbf[:], identf[0:16, 0:16]),
           reads=[("bbf",), ("identf",)], writes=[PS(7)])
    p.emit("scalar", lambda s: s.copy(bcol[:], ps[0:128, 3584:3584 + 16]),
           reads=[PS(7)], writes=[("bcol",)])

    def load_wblk(src, idx, slot):
        p.emit("gpsimd",
               lambda g, src=src, idx=idx, slot=slot: g.dma_start(
                   out=ws[:, 2048 * slot:2048 * (slot + 1)], in_=src[idx]),
               writes=[("ws", slot)], chain="ws%d" % slot)


    def P1a(u):
        tc0 = u * SEG
        p.emit("gpsimd",
               lambda g: g.dma_start(out=wvs[:, 0:2048], in_=wv[0]),
               writes=[("wvs", 0)], chain="wvs0")
        p.emit("gpsimd",
               [partial(lambda g, dt=dt, tc0=tc0: g.dma_start(
                   out=xnb[:, SEG * dt:SEG * (dt + 1)],
                   in_=xdt[128 * dt:128 * (dt + 1), tc0:tc0 + SEG]), dt=dt, tc0=tc0)
                for dt in range(8)],
               writes=[("xnb", dt) for dt in range(8)]
               + [ao_key(0, blk, h) for blk in range(16) for h in range(2)],
               chain="xb")
        yield
        for c in range(2):
            row = 32 * c
            bm = 4 + c
            mean_mm = []
            for dt in range(8):
                mean_mm.append(partial(
                    lambda t, dt, c, row, bm: t.matmul(
                        ps[row:row + 1, 512 * bm:512 * bm + 512],
                        oovr[:], xnb[:, SEG * dt + 512 * c: SEG * dt + 512 * (c + 1)],
                        start=(dt == 0), stop=False),
                    dt=dt, c=c, row=row, bm=bm))
            p.emit("tensor", mean_mm,
                   reads=[("oovr",)] + [("xnb", dt) for dt in range(8)],
                   writes=[PS(bm)])
            yield
        for dt in range(8):
            sl = dt % 2
            p.emit("vector",
                   lambda v, dt=dt, sl=sl: v.tensor_tensor(
                       sqb[:, SEG * sl:SEG * (sl + 1)],
                       xnb[:, SEG * dt:SEG * (dt + 1)],
                       xnb[:, SEG * dt:SEG * (dt + 1)], ALU.mult),
                   reads=[("xnb", dt)], writes=[("sq", sl)])
            for c in range(2):
                row = 32 * c
                bs = 6 + c
                p.emit("tensor",
                       lambda t, dt=dt, c=c, row=row, sl=sl, bs=bs: t.matmul(
                           ps[row:row + 1, 512 * bs:512 * bs + 512],
                           oovr[:], sqb[:, SEG * sl + 512 * c:SEG * sl + 512 * (c + 1)],
                           start=(dt == 0), stop=False),
                       reads=[("oovr",), ("sq", sl)], writes=[PS(bs)], accum=(dt > 0))
            yield

    def P1b(u):
        tc0 = u * SEG
        tmpb = denb
        p.emit("gpsimd",
               [partial(lambda g, dt=dt, tc0=tc0: g.dma_start(
                   out=xnb[:, SEG * dt:SEG * (dt + 1)],
                   in_=xdt[128 * dt:128 * (dt + 1), tc0:tc0 + SEG]), dt=dt, tc0=tc0)
                for dt in range(8, 16)],
               writes=[("xnb", dt) for dt in range(8, 16)]
               + [ao_key(1, blk, h) for blk in range(16) for h in range(2)],
               chain="xb2")
        for c in range(2):
            row = 32 * c
            bm = 4 + c
            mean_mm = []
            for dt in range(8, 16):
                mean_mm.append(partial(
                    lambda t, dt, c, row, bm: t.matmul(
                        ps[row:row + 1, 512 * bm:512 * bm + 512],
                        oovr[:], xnb[:, SEG * dt + 512 * c: SEG * dt + 512 * (c + 1)],
                        start=False, stop=(dt == ND - 1)),
                    dt=dt, c=c, row=row, bm=bm))
            p.emit("tensor", mean_mm,
                   reads=[("oovr",)] + [("xnb", dt) for dt in range(8, 16)],
                   writes=[PS(bm)], accum=True)
        for dt in range(8, 16):
            sl = dt % 2
            p.emit("vector",
                   lambda v, dt=dt, sl=sl: v.tensor_tensor(
                       sqb[:, SEG * sl:SEG * (sl + 1)],
                       xnb[:, SEG * dt:SEG * (dt + 1)],
                       xnb[:, SEG * dt:SEG * (dt + 1)], ALU.mult),
                   reads=[("xnb", dt)], writes=[("sq", sl)])
            for c in range(2):
                row = 32 * c
                bs = 6 + c
                p.emit("tensor",
                       lambda t, dt=dt, c=c, row=row, sl=sl, bs=bs: t.matmul(
                           ps[row:row + 1, 512 * bs:512 * bs + 512],
                           oovr[:], sqb[:, SEG * sl + 512 * c:SEG * sl + 512 * (c + 1)],
                           start=False, stop=(dt == ND - 1)),
                       reads=[("oovr",), ("sq", sl)], writes=[PS(bs)], accum=True)
        p.emit("scalar", [
            lambda s: s.copy(stA[0:1, :], ps[0:1, 512 * 4:512 * 5]),
            lambda s: s.copy(stA[32:33, :], ps[32:33, 512 * 5:512 * 6]),
        ], reads=[PS(4), PS(5)], writes=[("stA",)])
        p.emit("scalar", [
            lambda s: s.copy(stB[0:1, :], ps[0:1, 512 * 6:512 * 7]),
            lambda s: s.copy(stB[32:33, :], ps[32:33, 512 * 7:512 * 8]),
        ], reads=[PS(6), PS(7)], writes=[("stB",)])
        RR = (slice(0, 1), slice(32, 33))
        stC = pbds[:, 0:512]
        stD = pbr[:, 0:512]
        p.emit("vector", [partial(lambda v, r=r: v.tensor_tensor(
                   stC[r, :], stA[r, :], stA[r, :], ALU.mult)) for r in RR],
               reads=[("stA",)], writes=[("pbds", 0)])
        p.emit("vector", [partial(lambda v, r=r: v.tensor_tensor(
                   stD[r, :], stB[r, :], stC[r, :], ALU.subtract)) for r in RR],
               reads=[("stB",), ("pbds", 0)], writes=[("pbr", 0)])
        p.emit("scalar", [partial(lambda s, r=r: s.activation(
                   stC[r, :], stD[r, :], AF.Sqrt, bias=epsb[r, :])) for r in RR],
               reads=[("pbr", 0), ("epsb",)], writes=[("pbds", 0)])
        p.emit("vector", [partial(lambda v, r=r: v.reciprocal(
                   stD[r, :], stC[r, :])) for r in RR],
               reads=[("pbds", 0)], writes=[("pbr", 0)])
        p.emit("vector", [partial(lambda v, r=r: v.tensor_tensor(
                   stC[r, :], stA[r, :], stD[r, :], ALU.mult)) for r in RR],
               reads=[("stA",), ("pbr", 0)], writes=[("pbds", 0)])
        p.emit("vector", [partial(lambda v, r=r: v.tensor_copy(
                   stRS[r, :], stD[r, :])) for r in RR],
               reads=[("pbr", 0)], writes=[("stRS",)])
        p.emit("vector", [partial(lambda v, r=r: v.tensor_copy(
                   stMRS[r, :], stC[r, :])) for r in RR],
               reads=[("pbds", 0)], writes=[("stMRS",)])
        for c in range(2):
            row = 32 * c
            bsel = 4 + c
            p.emit("tensor",
                   lambda t, row=row, b0=bsel: t.matmul(
                       bank(b0), onesr[row:row + 1, :], stRS[row:row + 1, :],
                       start=True, stop=True),
                   reads=[("onesr",), ("stRS",)], writes=[PS(bsel)])
            p.emit("scalar",
                   lambda s, c=c, b0=bsel: s.copy(rsb[:, 512 * c:512 * (c + 1)], bank(b0)),
                   reads=[PS(bsel)], writes=[("rsb", c)])
            p.emit("tensor",
                   lambda t, row=row, b0=bsel: t.matmul(
                       bank(b0), onesr[row:row + 1, :], stMRS[row:row + 1, :],
                       start=True, stop=True),
                   reads=[("onesr",), ("stMRS",)], writes=[PS(bsel)])
            p.emit("scalar",
                   lambda s, c=c, b0=bsel: s.copy(mrsb[:, 512 * c:512 * (c + 1)], bank(b0)),
                   reads=[PS(bsel)], writes=[("mrsb", c)])
        for dt in range(ND):
            p.emit("vector",
                   lambda v, dt=dt: v.tensor_tensor(
                       tmpb[:], xnb[:, SEG * dt:SEG * (dt + 1)], rsb[:], ALU.mult),
                   reads=[("xnb", dt), ("rsb", 0), ("rsb", 1)],
                   writes=[("den", 0), ("den", 1)])
            p.emit("vector",
                   lambda v, dt=dt: v.tensor_tensor(
                       xnb[:, SEG * dt:SEG * (dt + 1)], tmpb[:], mrsb[:], ALU.subtract),
                   reads=[("den", 0), ("den", 1), ("mrsb", 0), ("mrsb", 1)], writes=[("xnb", dt)])

    def proj(src, not_, dst, dstkey, scale):
        for ot in range(not_):
            slot = ot % 4
            load_wblk(src, ot, slot)
            for c in range(2):
                b0 = (2 * ot + c) % 4
                mms = []
                for dt in range(ND):
                    mms.append(partial(
                        lambda t, c=c, dt=dt, slot=slot, b0=b0: t.matmul(
                            bank(b0),
                            ws[:, 2048 * slot + 128 * dt:2048 * slot + 128 * (dt + 1)],
                            xnb[:, SEG * dt + 512 * c:SEG * dt + 512 * (c + 1)],
                            start=(dt == 0), stop=(dt == ND - 1))))
                p.emit("tensor", mms,
                       reads=[("ws", slot)] + [("xnb", dt) for dt in range(ND)],
                       writes=[PS(b0)])
                if scale is None:
                    p.emit("scalar",
                           lambda s, ot=ot, c=c, b0=b0: s.copy(
                               dst[:, SEG * ot + 512 * c:SEG * ot + 512 * (c + 1)], bank(b0)),
                           reads=[PS(b0)], writes=[(dstkey, ot, c)])
                else:
                    p.emit("scalar",
                           lambda s, ot=ot, c=c, b0=b0, sc=scale: s.mul(
                               dst[:, SEG * ot + 512 * c:SEG * ot + 512 * (c + 1)], bank(b0), sc),
                           reads=[PS(b0)], writes=[(dstkey, ot, c)])

    def P2(u):
        for grp in range(2):
            tts = [4 * grp + i for i in range(4)]
            for dq in range(4):
                slot = dq % 2
                if not (grp == 0 and dq == 0):
                    p.emit("gpsimd",
                           lambda g, dq=dq, slot=slot: g.dma_start(
                               out=wvs[:, 2048 * slot:2048 * (slot + 1)], in_=wv[dq]),
                           writes=[("wvs", slot)], chain="wvs%d" % slot)
                for k in range(4):
                    dt = 4 * dq + k
                    for i, tt in enumerate(tts):
                        p.emit("tensor",
                               lambda t, dt=dt, tt=tt, i=i, slot=slot, k=k: t.matmul(
                                   bank(i),
                                   xnb[:, SEG * dt + 128 * tt:SEG * dt + 128 * (tt + 1)],
                                   wvs[:, 2048 * slot + 512 * k:2048 * slot + 512 * (k + 1)],
                                   start=(dt == 0), stop=(dt == ND - 1)),
                               reads=[("xnb", dt), ("wvs", slot)],
                               writes=[PS(i)], accum=(dt > 0))
            for i, tt in enumerate(tts):
                p.emit("scalar",
                       lambda s, tt=tt, i=i: s.copy(
                           VT[:, 768 * tt:768 * tt + 768]
                           .rearrange("q (p c) -> q p c", c=192)[:, :, 0:64],
                           bank(i).rearrange("q (p c) -> q p c", c=128)[:, :, 0:64]),
                       reads=[PS(i)], writes=[("VT", tt, 0)])
                p.emit("scalar",
                       lambda s, tt=tt, i=i: s.copy(
                           VT[:, 768 * tt:768 * tt + 768]
                           .rearrange("q (p c) -> q p c", c=192)[:, :, 128:192],
                           bank(i).rearrange("q (p c) -> q p c", c=128)[:, :, 64:128]),
                       reads=[PS(i)], writes=[("VT", tt, 1)])
        for tt in range(8):
            mms = []
            for dt in range(ND):
                mms.append(partial(
                    lambda t, tt=tt, dt=dt: t.matmul(
                        ps[:, 512 * 4 + 8 * tt:512 * 4 + 8 * (tt + 1)],
                        xnb[:, SEG * dt + 128 * tt:SEG * dt + 128 * (tt + 1)],
                        wrb[:, 8 * dt:8 * (dt + 1)],
                        start=(dt == 0), stop=(dt == ND - 1))))
            p.emit("tensor", mms,
                   reads=[("wrb",)] + [("xnb", dt) for dt in range(ND)],
                   writes=[PS(4)], accum=(tt > 0))
            p.emit("vector",
                   lambda v, tt=tt: v.reduce_max(
                       mxf[:, tt:tt + 1], ps[:, 512 * 4 + 8 * tt:512 * 4 + 8 * (tt + 1)], AX.X),
                   reads=[PS(4)], writes=[("mx", tt)])
            p.emit("vector",
                   lambda v, tt=tt: v.tensor_single_scalar(
                       mQf[:, 8 * tt:8 * (tt + 1)],
                       ps[:, 512 * 4 + 8 * tt:512 * 4 + 8 * (tt + 1)],
                       mxf[:, tt:tt + 1], ALU.is_equal),
                   reads=[PS(4), ("mx", tt)], writes=[("mQ", tt)])
            b0 = 5 + tt // 4
            co = 128 * (tt % 4)
            p.emit("tensor",
                   lambda t, tt=tt, b0=b0, co=co: t.transpose(
                       ps[0:8, 512 * b0 + co:512 * b0 + co + 128],
                       mQf[:, 8 * tt:8 * (tt + 1)], identf[:]),
                   reads=[("mQ", tt), ("identf",)], writes=[PS(b0)], accum=(tt % 4 > 0))
        for half in range(2):
            b0 = 5 + half
            p.emit("scalar",
                   lambda s, half=half, b0=b0: s.copy(
                       mTb[:, 512 * half:512 * (half + 1)], ps[0:8, 512 * b0:512 * (b0 + 1)]),
                   reads=[PS(b0)], writes=[("mT", half)])
            mms = []
            for q in range(4):
                mms.append(partial(
                    lambda t, half=half, q=q: t.matmul(
                        ps[:, 512 * 7 + 128 * q:512 * 7 + 128 * (q + 1)],
                        e16b[:], mTb[:, 512 * half + 128 * q:512 * half + 128 * (q + 1)],
                        start=True, stop=True)))
            p.emit("tensor", mms, reads=[("mT", half), ("e16b",)], writes=[PS(7)])
            p.emit("vector",
                   lambda v, half=half: v.tensor_copy(
                       mEb[:, 512 * half:512 * (half + 1)], bank(7)),
                   reads=[PS(7)], writes=[("mE", half)])
        for c in range(2):
            b0 = c
            mms = []
            for dt in range(ND):
                mms.append(partial(
                    lambda t, c=c, dt=dt, b0=b0: t.matmul(
                        bank(b0),
                        watb[:, 128 * dt:128 * (dt + 1)],
                        xnb[:, SEG * dt + 512 * c:SEG * dt + 512 * (c + 1)],
                        start=(dt == 0), stop=(dt == ND - 1))))
            p.emit("tensor", mms,
                   reads=[("watb",)] + [("xnb", dt) for dt in range(ND)],
                   writes=[PS(b0)])
            p.emit("scalar",
                   lambda s, c=c, b0=b0: s.copy(hb[:, 512 * c:512 * (c + 1)], bank(b0)),
                   reads=[PS(b0)], writes=[("h", c)])
            p.emit("vector",
                   lambda v, c=c: v.tensor_tensor(
                       hmb[:, 512 * c:512 * (c + 1)], hb[:, 512 * c:512 * (c + 1)],
                       mEb[:, 512 * c:512 * (c + 1)], ALU.mult),
                   reads=[("h", c), ("mE", c)], writes=[("hm", c)])
        proj(wq, 16, Q, "Q", 0.125)
        proj(wk, 4, K, "K", None)
        for ot in range(16):
            slot = 2 + ot % 2
            load_wblk(wb, ot, slot)
            for c in range(2):
                bG = 2 + (2 * ot + c) % 2
                mmg = []
                for dt in range(ND):
                    mmg.append(partial(
                        lambda t, dt=dt, c=c, slot=slot, bG=bG: t.matmul(
                            bank(bG),
                            ws[:, 2048 * slot + 128 * dt:2048 * slot + 128 * (dt + 1)],
                            xnb[:, SEG * dt + 512 * c:SEG * dt + 512 * (c + 1)],
                            start=(dt == 0), stop=False)))
                mmg.append(partial(
                    lambda t, ot=ot, c=c, bG=bG: t.matmul(
                        bank(bG),
                        wbctb[:, 128 * ot:128 * (ot + 1)],
                        hmb[:, 512 * c:512 * (c + 1)],
                        start=False, stop=True)))
                p.emit("tensor", mmg,
                       reads=[("ws", slot), ("wbctb",), ("hm", c)] + [("xnb", dt) for dt in range(ND)],
                       writes=[PS(bG)])
                p.emit("scalar",
                       lambda s, ot=ot, c=c, bG=bG: s.activation(
                           gb[:, SEG * ot + 512 * c:SEG * ot + 512 * (c + 1)], bank(bG),
                           AF.Silu, bias=bcol[:, ot:ot + 1]),
                       reads=[PS(bG), ("bcol",)], writes=[("g", ot, c)])

    def P3(u, qc, sc_banks, av_banks, pbd_bank, par0=None, esl_base=0, LA=3, nesl=4):
        nkt = 4 * qc + 4

        def make_normalize(blk, par, avE, avO):
            if par0 is not None:
                par = par0
            def n1():
                p.emit("vector", [
                    partial(lambda v, par=par, avE=avE: v.tensor_copy(
                        denb[64:65, 512 * par:512 * (par + 1)], ps[64:65, 512 * avE:512 * avE + 512])),
                    partial(lambda v, par=par, avO=avO: v.tensor_copy(
                        denb[0:1, 512 * par:512 * (par + 1)], ps[0:1, 512 * avO:512 * avO + 512])),
                ], reads=[PS(avE), PS(avO)], writes=[("den", par)])

            def n2():
                p.emit("tensor", [
                    partial(lambda t, par=par: t.matmul(
                        bank(pbd_bank, 0, 64), onesr[64:65, 0:64],
                        denb[64:65, 512 * par:512 * (par + 1)], start=True, stop=True)),
                    partial(lambda t, par=par: t.matmul(
                        bank(pbd_bank, 64, 128), onesr[0:1, 0:64],
                        denb[0:1, 512 * par:512 * (par + 1)], start=True, stop=True)),
                ], reads=[("den", par), ("onesr",)], writes=[PS(pbd_bank)])

            def n3():
                p.emit("scalar",
                       lambda s, par=par: s.copy(pbds[:, 512 * par:512 * (par + 1)], bank(pbd_bank)),
                       reads=[PS(pbd_bank)], writes=[("pbds", par)])
                p.emit("vector",
                       lambda v, par=par: v.reciprocal(
                           pbr[:, 512 * par:512 * (par + 1)], pbds[:, 512 * par:512 * (par + 1)]),
                       reads=[("pbds", par)], writes=[("pbr", par)])

            def n4():
                p.emit("vector", [
                    partial(lambda v, blk=blk, par=par, qc=qc, avE=avE: v.tensor_tensor(
                        AOsl(qc, 0, 64, blk),
                        ps[0:64, 512 * avE:512 * avE + 512],
                        pbr[0:64, 512 * par:512 * (par + 1)], ALU.mult)),
                    partial(lambda v, blk=blk, par=par, qc=qc, avO=avO: v.tensor_tensor(
                        AOsl(qc, 64, 128, blk),
                        ps[64:128, 512 * avO:512 * avO + 512],
                        pbr[64:128, 512 * par:512 * (par + 1)], ALU.mult)),
                ], reads=[PS(avE), PS(avO), ("pbr", par)],
                   writes=[ao_key(qc, blk, 0), ao_key(qc, blk, 1), ("xnb", 8 * qc + blk // 2)])

            def n23():
                n2()
                n3()
            return [n1, n23, n4]

        def emit_scores(kp, blk, i, bE, bO):
            mmE = partial(
                lambda t, kp=kp, i=i, blk=blk, qc=qc, bE=bE: t.matmul(
                    bank(bE),
                    K[0:64, SEG * kp + 128 * i:SEG * kp + 128 * (i + 1)],
                    Q[0:64, SEG * blk + 512 * qc:SEG * blk + 512 * (qc + 1)],
                    start=True, stop=True))
            mmO = partial(
                lambda t, kp=kp, i=i, blk=blk, qc=qc, bO=bO: t.matmul(
                    bank(bO),
                    K[64:128, SEG * kp + 128 * i:SEG * kp + 128 * (i + 1)],
                    Q[64:128, SEG * blk + 512 * qc:SEG * blk + 512 * (qc + 1)],
                    start=True, stop=True))
            p.emit("tensor", [mmE, mmO],
                   reads=[("K", kp, i // 4), ("Q", blk, qc)],
                   writes=[PS(bE), PS(bO)])

        def emit_mask(esl, i, off):
            o = i - 4 * qc
            if o < 0:
                return
            p.emit("vector",
                   lambda v, esl=esl, off=off, o=o: v.tensor_tensor(
                       Eb[:, 1024 * esl + off:1024 * esl + off + 1024]
                       .rearrange("p (a c) -> p a c", a=2),
                       Eb[:, 1024 * esl + off:1024 * esl + off + 1024]
                       .rearrange("p (a c) -> p a c", a=2),
                       masksb[:, 512 * o:512 * (o + 1)]
                       .rearrange("p (a c) -> p a c", a=1).broadcast_to([128, 2, 512]),
                       ALU.mult),
                   reads=[("E", esl), ("masks",)], writes=[("E", esl)])

        def emit_av(kp, i, esl, off, first, last, avE, avO):
            avmE = partial(
                lambda t, kp=kp, i=i, esl=esl, off=off, first=first, last=last, avE=avE: t.matmul(
                    bank(avE),
                    VT[:, 768 * i + 192 * kp:768 * i + 192 * kp + 128],
                    Eb[:, 1024 * esl + off:1024 * esl + off + 512],
                    start=first, stop=last))
            avmO = partial(
                lambda t, kp=kp, i=i, esl=esl, off=off, first=first, last=last, avO=avO: t.matmul(
                    bank(avO),
                    VT[:, 768 * i + 192 * kp + 64:768 * i + 192 * kp + 192],
                    Eb[:, 1024 * esl + off + 512:1024 * esl + off + 1024],
                    start=first, stop=last))
            p.emit("tensor", [avmE, avmO],
                   reads=[("VT", i, 0), ("VT", i, 1), ("vtc",), ("E", esl)],
                   writes=[PS(avE), PS(avO)], accum=(not first))

        pending = []
        for kp in range(4):
            for j in range(4):
                blk = 4 * kp + j
                avE, avO = av_banks[(4 * kp + j) % len(av_banks)]
                for step in range(nkt + LA):
                    if step < nkt:
                        i = step
                        bE = sc_banks[2 * (i % 2)]
                        bO = sc_banks[2 * (i % 2) + 1]
                        esl = esl_base + i % nesl
                        emit_scores(kp, blk, i, bE, bO)
                        p.emit("scalar",
                               lambda s, esl=esl, bE=bE: s.activation(
                                   Eb[:, 1024 * esl:1024 * (esl + 1)],
                                   ps[:, 512 * bE:512 * bE + 1024], AF.Exp),
                               reads=[PS(bE), PS(bE + 1)], writes=[("E", esl)])
                        emit_mask(esl, i, 0)
                    if step >= LA:
                        i = step - LA
                        esl = esl_base + i % nesl
                        emit_av(kp, i, esl, 0, i == 0, i == nkt - 1, avE, avO)
                    if step >= 1 and step % 2 == 1 and pending:
                        pending.pop(0)()
                    yield
                if len(av_banks) > 1:
                    while pending:
                        pending.pop(0)()
                    pending = list(make_normalize(blk, j % 2, avE, avO))
                else:
                    for fn in make_normalize(blk, j % 2, avE, avO):
                        fn()
                    yield
        while pending:
            pending.pop(0)()
            yield

    def P4(u, qc, sbanks, wslots):
        tc0 = u * SEG
        ao_reads = [ao_key(qc, dtb, hf) for dtb in range(16) for hf in range(2)]
        for ot in range(16):
            slot = wslots[ot % len(wslots)]
            load_wblk(wo, ot, slot)
            bS = sbanks[ot % 2]
            p.emit("sync",
                   lambda s, ot=ot, qc=qc, slot=ot % 2, tc0=tc0: s.dma_start(
                       out=xre[:, 512 * slot:512 * (slot + 1)],
                       in_=xdt[128 * ot:128 * (ot + 1), tc0 + 512 * qc:tc0 + 512 * (qc + 1)]),
                   writes=[("xre", ot % 2)], chain="xre%d" % (ot % 2))
            for piece in range(8):
                mms = []
                for dt in range(2 * piece, 2 * piece + 2):
                    mms.append(partial(
                        lambda t, dt=dt, slot=slot, bS=bS, qc=qc: t.matmul(
                            bank(bS),
                            ws[:, 2048 * slot + 128 * dt:2048 * slot + 128 * (dt + 1)],
                            AOsl(qc, 0, 128, dt),
                            start=(dt == 0), stop=(dt == ND - 1))))
                p.emit("tensor", mms,
                       reads=[("ws", slot)] + ao_reads,
                       writes=[PS(bS)], accum=(piece > 0))
                yield
            p.emit("vector",
                   lambda v, ot=ot, qc=qc, slot=ot % 2, bS=bS: v.tensor_tensor(
                       sgf[:, 512 * slot:512 * (slot + 1)], bank(bS),
                       gb[:, SEG * ot + 512 * qc:SEG * ot + 512 * (qc + 1)], ALU.mult),
                   reads=[PS(bS), ("g", ot, qc)], writes=[("sg", ot % 2)])
            p.emit("vector",
                   lambda v, slot=ot % 2: v.tensor_tensor(
                       stg[:, 512 * slot:512 * (slot + 1)],
                       sgf[:, 512 * slot:512 * (slot + 1)],
                       xre[:, 512 * slot:512 * (slot + 1)], ALU.add),
                   reads=[("sg", ot % 2), ("xre", ot % 2)], writes=[("stg", ot % 2)])
            p.emit("sync",
                   lambda s, ot=ot, qc=qc, slot=ot % 2, tc0=tc0: s.dma_start(
                       out=outT[128 * ot:128 * (ot + 1), tc0 + 512 * qc:tc0 + 512 * (qc + 1)],
                       in_=stg[:, 512 * slot:512 * (slot + 1)]),
                   reads=[("stg", ot % 2)], chain="st%d" % (ot % 2))
            yield

    prevP4 = None
    for u in units:
        if prevP4 is not None:
            drive([(prevP4, 2), (P1a(u), 1)])
        else:
            drive([(P1a(u), 1)])
        P1b(u)
        P2(u)
        drive([(P3(u, 0, (0, 1, 2, 3), [(4, 5), (6, 7)], 0), 1)])
        drive([(P3(u, 1, (0, 1, 2, 3), [(4, 5)], 0), 1),
               (P4(u, 0, (6, 7), (0, 1, 2, 3)), 1)])
        prevP4 = P4(u, 1, (0, 1), (0, 1, 2, 3))
    if prevP4 is not None:
        drive([(prevP4, 1)])

    for sl in range(2):
        name = "st%d" % sl
        if name in p.dchain:
            p.raw_wait("sync", ("d", name), 16 * p.dchain[name])

    import os
    if os.environ.get("KPROBE_SBUF"):
        print("SBUF remaining B/part:", nc.sbuf_bytes_remaining)
    p.replay()
    ctx.close()
    return nc


def _host_prep(inputs):
    f32 = np.float32
    x = np.asarray(inputs["L0_attn_hidden_states"], dtype=f32)
    Wq = np.asarray(inputs["L0_attn_Wq"], dtype=f32)
    Wk = np.asarray(inputs["L0_attn_Wk"], dtype=f32)
    Wv = np.asarray(inputs["L0_attn_Wv"], dtype=f32)
    Wo = np.asarray(inputs["L0_attn_Wo"], dtype=f32)
    Wb = np.asarray(inputs["L0_ffn_W_base"], dtype=f32)
    bb = np.asarray(inputs["L0_ffn_b_base"], dtype=f32)
    rW = np.asarray(inputs["L0_ffn_router_W"], dtype=f32)
    A = np.asarray(inputs["L0_ffn_A"], dtype=f32)
    Bw = np.asarray(inputs["L0_ffn_Bw"], dtype=f32)

    qperm = np.empty(D, dtype=np.int64)
    for blk in range(16):
        for s in range(2):
            h = 8 * (blk // 4) + 4 * s + (blk % 4)
            qperm[128 * blk + 64 * s: 128 * blk + 64 * s + 64] = np.arange(64) + 64 * h

    def blocks(W):
        return np.ascontiguousarray(
            W.reshape(16, 128, 16, 128).transpose(0, 3, 2, 1).reshape(16, 128, 2048))

    wq_h = blocks(Wq[qperm, :])
    wo_h = blocks(Wo[:, qperm])
    wb_h = blocks(Wb)
    wk_h = np.ascontiguousarray(Wk.reshape(4, 128, 16, 128).transpose(0, 3, 2, 1).reshape(4, 128, 2048))
    wv_h = np.ascontiguousarray(Wv.T.reshape(4, 4, 128, 512).transpose(0, 2, 1, 3).reshape(4, 128, 2048))
    wr_h = np.ascontiguousarray(rW.T.reshape(16, 128, 8).transpose(1, 0, 2).reshape(128, 128))
    wat_h = np.ascontiguousarray(
        A.reshape(128, 2048).T.reshape(16, 128, 128).transpose(1, 0, 2).reshape(128, 2048))
    wbct_h = np.ascontiguousarray(Bw.transpose(0, 2, 1).reshape(128, 2048))
    bb_h = np.ascontiguousarray(bb.reshape(16, 128))
    e16_h = np.zeros((8, 128), dtype=f32)
    for ex in range(8):
        e16_h[ex, 16 * ex:16 * (ex + 1)] = 2.0
    ident_h = np.eye(128, dtype=f32)
    masks_h = np.zeros((128, 2048), dtype=f32)
    for o in range(4):
        masks_h[:, 512 * o:512 * (o + 1)] = (
            np.arange(512)[None, :] >= (128 * o + np.arange(128))[:, None]).astype(f32)
    x_dt = np.ascontiguousarray(x.transpose(0, 2, 1))

    shared = {
        "wq": wq_h, "wk": wk_h, "wv": wv_h, "wo": wo_h, "wb": wb_h,
        "wr": wr_h, "wat": wat_h, "wbct": wbct_h, "bb": bb_h,
        "e16": e16_h, "ident": ident_h, "masks": masks_h,
    }
    return x_dt, shared


def kernel(**inputs):
    x_dt, shared = _host_prep(inputs)
    nc = build_nc()
    in_maps = []
    for c in range(B):
        m = dict(shared)
        m["x_dt"] = np.ascontiguousarray(x_dt[c])
        in_maps.append(m)
    res = run_bass_kernel_spmd(nc, in_maps, core_ids=list(range(B)))
    out = np.stack([np.asarray(r["out_T"]) for r in res.results])
    return np.ascontiguousarray(out.transpose(0, 2, 1)).astype(np.float32)


# version 35 (final)
# speedup vs baseline: 1.2296x; 1.2296x over previous
import numpy as np
import ml_dtypes
from contextlib import ExitStack
from functools import partial

import concourse.bass as bass
import concourse.mybir as mybir
from concourse.bass_utils import run_bass_kernel_spmd

F32 = mybir.dt.float32
BF16 = mybir.dt.bfloat16
AF = mybir.ActivationFunctionType
ALU = mybir.AluOpType
AX = mybir.AxisListType

B = 8
D = 2048
T = 2048
SEG = 1024
ND = 16
EPS = 1e-5

ENGINES = ("tensor", "vector", "scalar", "sync", "gpsimd")


class Prog:

    def __init__(self, nc):
        self.nc = nc
        self.stream = {e: [] for e in ENGINES}
        self.cnt = {e: 0 for e in ENGINES}
        self.sems = {}
        for e in ENGINES:
            self.sems[("e", e)] = nc.alloc_semaphore(f"s_{e}")
        self.dchain = {}
        self.bufs = {}
        self.hw = {e: {} for e in ENGINES}

    def chain(self, name):
        if name not in self.dchain:
            self.sems[("d", name)] = self.nc.alloc_semaphore(f"d_{name}")
            self.dchain[name] = 0

    def emit(self, eng, fns, reads=(), writes=(), chain=None, accum=False):
        if callable(fns):
            fns = [fns]
        agg = {}

        def need(ev):
            if ev is None:
                return
            kind, nm, val = ev
            key = (kind, nm)
            if agg.get(key, 0) < val:
                agg[key] = val

        for k in reads:
            b = self.bufs.get(k)
            if b:
                need(b[0])
        for k in writes:
            b = self.bufs.get(k)
            if b:
                if not accum:
                    need(b[0])
                for ev in b[1]:
                    need(ev)
        for (kind, nm), val in sorted(agg.items(), key=lambda x: (x[0][0], str(x[0][1]), x[1])):
            cur = self.hw[eng].get((kind, nm), 0)
            if val > cur:
                self.stream[eng].append(("w", (kind, nm), val))
                self.hw[eng][(kind, nm)] = val
        if chain is not None:
            self.chain(chain)
            self.dchain[chain] += len(fns)
            ev = ("d", chain, 16 * self.dchain[chain])
            self.stream[eng].append(("i", fns, ("d", chain), 16, True))
        else:
            self.cnt[eng] += 1
            ev = ("e", eng, self.cnt[eng])
            self.stream[eng].append(("i", fns, ("e", eng), 1, False))
        for k in writes:
            self.bufs[k] = [ev, []]
        for k in reads:
            b = self.bufs.setdefault(k, [None, []])
            b[1].append(ev)
        return ev

    def raw_wait(self, eng, semkey, val):
        cur = self.hw[eng].get(semkey, 0)
        if val > cur:
            self.stream[eng].append(("w", semkey, val))
            self.hw[eng][semkey] = val

    def replay(self):
        with self.nc.Block() as block:
            for eng in ENGINES:
                items = self.stream[eng]
                if not items:
                    continue

                def body(h, items=items):
                    for it in items:
                        if it[0] == "w":
                            _, key, val = it
                            h.wait_ge(self.sems[key], val)
                        else:
                            _, fns, semkey, inc, inc_each = it
                            if inc_each:
                                for f in fns:
                                    f(h).then_inc(self.sems[semkey], inc)
                            else:
                                last = None
                                for f in fns:
                                    last = f(h)
                                last.then_inc(self.sems[semkey], inc)

                getattr(block, eng)(body)


def drive(tasks):
    live = [[gen, w] for gen, w in tasks]
    while live:
        for t in list(live):
            gen, w = t
            try:
                for _ in range(w):
                    next(gen)
            except StopIteration:
                live.remove(t)


def build_nc(units=(0, 1)):
    nc = bass.Bass("TRN2", target_bir_lowering=False)

    xdt = nc.dram_tensor("x_dt", [D, T], F32, kind="ExternalInput").ap()
    wq = nc.dram_tensor("wq", [16, 128, 2048], F32, kind="ExternalInput").ap()
    wk = nc.dram_tensor("wk", [4, 128, 2048], F32, kind="ExternalInput").ap()
    wv = nc.dram_tensor("wv", [4, 128, 2048], F32, kind="ExternalInput").ap()
    wo = nc.dram_tensor("wo", [16, 128, 2048], F32, kind="ExternalInput").ap()
    wb = nc.dram_tensor("wb", [16, 128, 2048], F32, kind="ExternalInput").ap()
    wr = nc.dram_tensor("wr", [128, 128], F32, kind="ExternalInput").ap()
    wat = nc.dram_tensor("wat", [128, 2048], F32, kind="ExternalInput").ap()
    wbct = nc.dram_tensor("wbct", [128, 2048], F32, kind="ExternalInput").ap()
    bbh = nc.dram_tensor("bb", [16, 128], F32, kind="ExternalInput").ap()
    e16h = nc.dram_tensor("e16", [8, 128], F32, kind="ExternalInput").ap()
    identh = nc.dram_tensor("ident", [128, 128], F32, kind="ExternalInput").ap()
    masksh = nc.dram_tensor("masks", [128, 2048], F32, kind="ExternalInput").ap()
    outT = nc.dram_tensor("out_T", [D, T], F32, kind="ExternalOutput").ap()

    ctx = ExitStack()
    e = ctx.enter_context

    xnb = e(nc.sbuf_tensor("xnb", [128, ND * SEG], BF16))
    sqb = e(nc.sbuf_tensor("sqb", [128, 2 * SEG], BF16))
    Q = e(nc.sbuf_tensor("Q", [128, 16 * SEG], BF16))
    K = e(nc.sbuf_tensor("K", [128, 4 * SEG], BF16))
    VT = e(nc.sbuf_tensor("VT", [128, 8 * 768], BF16))
    gb = e(nc.sbuf_tensor("gb", [128, 16 * SEG], BF16))
    Eb = e(nc.sbuf_tensor("Eb", [128, 2 * 2048], BF16))
    hb = e(nc.sbuf_tensor("hb", [128, SEG], BF16))
    hmb = e(nc.sbuf_tensor("hmb", [128, SEG], BF16))
    mEb = e(nc.sbuf_tensor("mEb", [128, SEG], BF16))
    mQf = e(nc.sbuf_tensor("mQf", [128, 64], F32))
    mxf = e(nc.sbuf_tensor("mxf", [128, 16], F32))
    mTb = e(nc.sbuf_tensor("mTb", [8, 1024], BF16))
    denb = e(nc.sbuf_tensor("denb", [128, 2 * 512], BF16))
    pbds = e(nc.sbuf_tensor("pbds", [128, 2 * 512], F32))
    pbr = e(nc.sbuf_tensor("pbr", [128, 2 * 512], F32))
    stA = e(nc.sbuf_tensor("stA", [128, 512], F32))
    stB = e(nc.sbuf_tensor("stB", [128, 512], F32))
    stRS = e(nc.sbuf_tensor("stRS", [128, 512], BF16))
    stMRS = e(nc.sbuf_tensor("stMRS", [128, 512], BF16))
    rsb = e(nc.sbuf_tensor("rsb", [128, SEG], BF16))
    mrsb = e(nc.sbuf_tensor("mrsb", [128, SEG], BF16))
    ws = e(nc.sbuf_tensor("ws", [128, 4 * 2048], BF16))
    wvs = e(nc.sbuf_tensor("wvs", [128, 2 * 2048], BF16))
    wrb = e(nc.sbuf_tensor("wrb", [128, 128], BF16))
    watb = e(nc.sbuf_tensor("watb", [128, 2048], BF16))
    wbctb = e(nc.sbuf_tensor("wbctb", [128, 2048], BF16))
    e16b = e(nc.sbuf_tensor("e16b", [8, 128], BF16))
    identf = e(nc.sbuf_tensor("identf", [128, 128], F32))
    masksb = e(nc.sbuf_tensor("masksb", [128, 2048], BF16))
    bbf = e(nc.sbuf_tensor("bbf", [16, 128], F32))
    bcol = e(nc.sbuf_tensor("bcol", [128, 16], F32))
    onesr = e(nc.sbuf_tensor("onesr", [128, 128], BF16))
    epsb = e(nc.sbuf_tensor("epsb", [128, 1], F32))
    oovr = e(nc.sbuf_tensor("oovr", [128, 1], BF16))
    xre = e(nc.sbuf_tensor("xre", [128, 2 * 512], F32))
    stg = e(nc.sbuf_tensor("stg", [128, 2 * 512], F32))
    sgf = e(nc.sbuf_tensor("sgf", [128, 2 * 512], F32))
    ps = e(nc.psum_tensor("ps", [128, 4096], F32))

    p = Prog(nc)

    def bank(b0, p0=0, p1=128, c0=0, w=512):
        return ps[p0:p1, 512 * b0 + c0:512 * b0 + c0 + w]

    PS = lambda b0: ("ps", b0)

    def AOsl(qc, r0, r1, blk):
        base = 8192 * qc + 512 * blk
        return xnb[r0:r1, base:base + 512]

    def ao_key(qc, blk, half):
        return ("AO", qc, blk, half)

    ALL_AO = [ao_key(qc, blk, h) for qc in range(2) for blk in range(16) for h in range(2)]

    p.emit("gpsimd", [
        lambda g: g.dma_start(out=wrb[:], in_=wr),
        lambda g: g.dma_start(out=watb[:], in_=wat),
        lambda g: g.dma_start(out=wbctb[:], in_=wbct),
        lambda g: g.dma_start(out=e16b[:], in_=e16h),
        lambda g: g.dma_start(out=masksb[:], in_=masksh),
    ], writes=[("wrb",), ("watb",), ("wbctb",), ("e16b",), ("masks",)], chain="cst")
    p.emit("sync", [
        lambda s: s.dma_start(out=identf[:], in_=identh),
        lambda s: s.dma_start(out=bbf[:], in_=bbh),
    ], writes=[("identf",), ("bbf",)], chain="sct")
    p.emit("vector", lambda v: v.memset(onesr[:], 1.0), writes=[("onesr",)])
    p.emit("vector", lambda v: v.memset(epsb[:], EPS), writes=[("epsb",)])
    p.emit("vector", lambda v: v.memset(oovr[:], 1.0 / 2048.0), writes=[("oovr",)])
    vt3 = VT[:].rearrange("q (t p c) -> q t p c", p=4, c=192)
    p.emit("vector", lambda v: v.memset(vt3[:, :, :, 64:65], 1.0), writes=[("vtc",)])
    p.emit("vector", lambda v: v.memset(vt3[:, :, :, 65:128], 0.0), writes=[("vtc",)])
    p.emit("tensor", lambda t: t.transpose(ps[0:128, 3584:3584 + 16], bbf[:], identf[0:16, 0:16]),
           reads=[("bbf",), ("identf",)], writes=[PS(7)])
    p.emit("scalar", lambda s: s.copy(bcol[:], ps[0:128, 3584:3584 + 16]),
           reads=[PS(7)], writes=[("bcol",)])

    def load_wblk(src, idx, slot):
        p.emit("gpsimd",
               lambda g, src=src, idx=idx, slot=slot: g.dma_start(
                   out=ws[:, 2048 * slot:2048 * (slot + 1)], in_=src[idx]),
               writes=[("ws", slot)], chain="ws%d" % slot)


    def P1a(u):
        tc0 = u * SEG
        p.emit("gpsimd",
               lambda g: g.dma_start(out=wvs[:, 0:2048], in_=wv[0]),
               writes=[("wvs", 0)], chain="wvs0")
        p.emit("gpsimd",
               [partial(lambda g, dt=dt, tc0=tc0: g.dma_start(
                   out=xnb[:, SEG * dt:SEG * (dt + 1)],
                   in_=xdt[128 * dt:128 * (dt + 1), tc0:tc0 + SEG]), dt=dt, tc0=tc0)
                for dt in range(8)],
               writes=[("xnb", dt) for dt in range(8)]
               + [ao_key(0, blk, h) for blk in range(16) for h in range(2)],
               chain="xb")
        yield
        for c in range(2):
            row = 32 * c
            bm = 4 + c
            mean_mm = []
            for dt in range(8):
                mean_mm.append(partial(
                    lambda t, dt, c, row, bm: t.matmul(
                        ps[row:row + 1, 512 * bm:512 * bm + 512],
                        oovr[:], xnb[:, SEG * dt + 512 * c: SEG * dt + 512 * (c + 1)],
                        start=(dt == 0), stop=False),
                    dt=dt, c=c, row=row, bm=bm))
            p.emit("tensor", mean_mm,
                   reads=[("oovr",)] + [("xnb", dt) for dt in range(8)],
                   writes=[PS(bm)])
            yield
        for dt in range(8):
            sl = dt % 2
            p.emit("vector",
                   lambda v, dt=dt, sl=sl: v.tensor_tensor(
                       sqb[:, SEG * sl:SEG * (sl + 1)],
                       xnb[:, SEG * dt:SEG * (dt + 1)],
                       xnb[:, SEG * dt:SEG * (dt + 1)], ALU.mult),
                   reads=[("xnb", dt)], writes=[("sq", sl)])
            for c in range(2):
                row = 32 * c
                bs = 6 + c
                p.emit("tensor",
                       lambda t, dt=dt, c=c, row=row, sl=sl, bs=bs: t.matmul(
                           ps[row:row + 1, 512 * bs:512 * bs + 512],
                           oovr[:], sqb[:, SEG * sl + 512 * c:SEG * sl + 512 * (c + 1)],
                           start=(dt == 0), stop=False),
                       reads=[("oovr",), ("sq", sl)], writes=[PS(bs)], accum=(dt > 0))
            yield

    def P1b(u):
        tc0 = u * SEG
        tmpb = denb
        p.emit("gpsimd",
               [partial(lambda g, dt=dt, tc0=tc0: g.dma_start(
                   out=xnb[:, SEG * dt:SEG * (dt + 1)],
                   in_=xdt[128 * dt:128 * (dt + 1), tc0:tc0 + SEG]), dt=dt, tc0=tc0)
                for dt in range(8, 16)],
               writes=[("xnb", dt) for dt in range(8, 16)]
               + [ao_key(1, blk, h) for blk in range(16) for h in range(2)],
               chain="xb2")
        for c in range(2):
            row = 32 * c
            bm = 4 + c
            mean_mm = []
            for dt in range(8, 16):
                mean_mm.append(partial(
                    lambda t, dt, c, row, bm: t.matmul(
                        ps[row:row + 1, 512 * bm:512 * bm + 512],
                        oovr[:], xnb[:, SEG * dt + 512 * c: SEG * dt + 512 * (c + 1)],
                        start=False, stop=(dt == ND - 1)),
                    dt=dt, c=c, row=row, bm=bm))
            p.emit("tensor", mean_mm,
                   reads=[("oovr",)] + [("xnb", dt) for dt in range(8, 16)],
                   writes=[PS(bm)], accum=True)
        for dt in range(8, 16):
            sl = dt % 2
            p.emit("vector",
                   lambda v, dt=dt, sl=sl: v.tensor_tensor(
                       sqb[:, SEG * sl:SEG * (sl + 1)],
                       xnb[:, SEG * dt:SEG * (dt + 1)],
                       xnb[:, SEG * dt:SEG * (dt + 1)], ALU.mult),
                   reads=[("xnb", dt)], writes=[("sq", sl)])
            for c in range(2):
                row = 32 * c
                bs = 6 + c
                p.emit("tensor",
                       lambda t, dt=dt, c=c, row=row, sl=sl, bs=bs: t.matmul(
                           ps[row:row + 1, 512 * bs:512 * bs + 512],
                           oovr[:], sqb[:, SEG * sl + 512 * c:SEG * sl + 512 * (c + 1)],
                           start=False, stop=(dt == ND - 1)),
                       reads=[("oovr",), ("sq", sl)], writes=[PS(bs)], accum=True)
        p.emit("scalar", [
            lambda s: s.copy(stA[0:1, :], ps[0:1, 512 * 4:512 * 5]),
            lambda s: s.copy(stA[32:33, :], ps[32:33, 512 * 5:512 * 6]),
        ], reads=[PS(4), PS(5)], writes=[("stA",)])
        p.emit("scalar", [
            lambda s: s.copy(stB[0:1, :], ps[0:1, 512 * 6:512 * 7]),
            lambda s: s.copy(stB[32:33, :], ps[32:33, 512 * 7:512 * 8]),
        ], reads=[PS(6), PS(7)], writes=[("stB",)])
        RR = (slice(0, 1), slice(32, 33))
        stC = pbds[:, 0:512]
        stD = pbr[:, 0:512]
        p.emit("vector", [partial(lambda v, r=r: v.tensor_tensor(
                   stC[r, :], stA[r, :], stA[r, :], ALU.mult)) for r in RR],
               reads=[("stA",)], writes=[("pbds", 0)])
        p.emit("vector", [partial(lambda v, r=r: v.tensor_tensor(
                   stD[r, :], stB[r, :], stC[r, :], ALU.subtract)) for r in RR],
               reads=[("stB",), ("pbds", 0)], writes=[("pbr", 0)])
        p.emit("scalar", [partial(lambda s, r=r: s.activation(
                   stC[r, :], stD[r, :], AF.Sqrt, bias=epsb[r, :])) for r in RR],
               reads=[("pbr", 0), ("epsb",)], writes=[("pbds", 0)])
        p.emit("vector", [partial(lambda v, r=r: v.reciprocal(
                   stD[r, :], stC[r, :])) for r in RR],
               reads=[("pbds", 0)], writes=[("pbr", 0)])
        p.emit("vector", [partial(lambda v, r=r: v.tensor_tensor(
                   stC[r, :], stA[r, :], stD[r, :], ALU.mult)) for r in RR],
               reads=[("stA",), ("pbr", 0)], writes=[("pbds", 0)])
        p.emit("vector", [partial(lambda v, r=r: v.tensor_copy(
                   stRS[r, :], stD[r, :])) for r in RR],
               reads=[("pbr", 0)], writes=[("stRS",)])
        p.emit("vector", [partial(lambda v, r=r: v.tensor_copy(
                   stMRS[r, :], stC[r, :])) for r in RR],
               reads=[("pbds", 0)], writes=[("stMRS",)])
        for c in range(2):
            row = 32 * c
            bsel = 4 + c
            p.emit("tensor",
                   lambda t, row=row, b0=bsel: t.matmul(
                       bank(b0), onesr[row:row + 1, :], stRS[row:row + 1, :],
                       start=True, stop=True),
                   reads=[("onesr",), ("stRS",)], writes=[PS(bsel)])
            p.emit("scalar",
                   lambda s, c=c, b0=bsel: s.copy(rsb[:, 512 * c:512 * (c + 1)], bank(b0)),
                   reads=[PS(bsel)], writes=[("rsb", c)])
            p.emit("tensor",
                   lambda t, row=row, b0=bsel: t.matmul(
                       bank(b0), onesr[row:row + 1, :], stMRS[row:row + 1, :],
                       start=True, stop=True),
                   reads=[("onesr",), ("stMRS",)], writes=[PS(bsel)])
            p.emit("scalar",
                   lambda s, c=c, b0=bsel: s.copy(mrsb[:, 512 * c:512 * (c + 1)], bank(b0)),
                   reads=[PS(bsel)], writes=[("mrsb", c)])
        for dt in range(ND):
            p.emit("vector",
                   lambda v, dt=dt: v.tensor_tensor(
                       tmpb[:], xnb[:, SEG * dt:SEG * (dt + 1)], rsb[:], ALU.mult),
                   reads=[("xnb", dt), ("rsb", 0), ("rsb", 1)],
                   writes=[("den", 0), ("den", 1)])
            p.emit("vector",
                   lambda v, dt=dt: v.tensor_tensor(
                       xnb[:, SEG * dt:SEG * (dt + 1)], tmpb[:], mrsb[:], ALU.subtract),
                   reads=[("den", 0), ("den", 1), ("mrsb", 0), ("mrsb", 1)], writes=[("xnb", dt)])

    def proj(src, not_, dst, dstkey, scale):
        for ot in range(not_):
            slot = ot % 4
            load_wblk(src, ot, slot)
            for c in range(2):
                b0 = (2 * ot + c) % 4
                mms = []
                for dt in range(ND):
                    mms.append(partial(
                        lambda t, c=c, dt=dt, slot=slot, b0=b0: t.matmul(
                            bank(b0),
                            ws[:, 2048 * slot + 128 * dt:2048 * slot + 128 * (dt + 1)],
                            xnb[:, SEG * dt + 512 * c:SEG * dt + 512 * (c + 1)],
                            start=(dt == 0), stop=(dt == ND - 1))))
                p.emit("tensor", mms,
                       reads=[("ws", slot)] + [("xnb", dt) for dt in range(ND)],
                       writes=[PS(b0)])
                if scale is None:
                    p.emit("scalar",
                           lambda s, ot=ot, c=c, b0=b0: s.copy(
                               dst[:, SEG * ot + 512 * c:SEG * ot + 512 * (c + 1)], bank(b0)),
                           reads=[PS(b0)], writes=[(dstkey, ot, c)])
                else:
                    p.emit("scalar",
                           lambda s, ot=ot, c=c, b0=b0, sc=scale: s.mul(
                               dst[:, SEG * ot + 512 * c:SEG * ot + 512 * (c + 1)], bank(b0), sc),
                           reads=[PS(b0)], writes=[(dstkey, ot, c)])

    def P2(u):
        for grp in range(2):
            tts = [4 * grp + i for i in range(4)]
            for dq in range(4):
                slot = dq % 2
                if not (grp == 0 and dq == 0):
                    p.emit("gpsimd",
                           lambda g, dq=dq, slot=slot: g.dma_start(
                               out=wvs[:, 2048 * slot:2048 * (slot + 1)], in_=wv[dq]),
                           writes=[("wvs", slot)], chain="wvs%d" % slot)
                for k in range(4):
                    dt = 4 * dq + k
                    for i, tt in enumerate(tts):
                        p.emit("tensor",
                               lambda t, dt=dt, tt=tt, i=i, slot=slot, k=k: t.matmul(
                                   bank(i),
                                   xnb[:, SEG * dt + 128 * tt:SEG * dt + 128 * (tt + 1)],
                                   wvs[:, 2048 * slot + 512 * k:2048 * slot + 512 * (k + 1)],
                                   start=(dt == 0), stop=(dt == ND - 1)),
                               reads=[("xnb", dt), ("wvs", slot)],
                               writes=[PS(i)], accum=(dt > 0))
            for i, tt in enumerate(tts):
                p.emit("scalar",
                       lambda s, tt=tt, i=i: s.copy(
                           VT[:, 768 * tt:768 * tt + 768]
                           .rearrange("q (p c) -> q p c", c=192)[:, :, 0:64],
                           bank(i).rearrange("q (p c) -> q p c", c=128)[:, :, 0:64]),
                       reads=[PS(i)], writes=[("VT", tt, 0)])
                p.emit("scalar",
                       lambda s, tt=tt, i=i: s.copy(
                           VT[:, 768 * tt:768 * tt + 768]
                           .rearrange("q (p c) -> q p c", c=192)[:, :, 128:192],
                           bank(i).rearrange("q (p c) -> q p c", c=128)[:, :, 64:128]),
                       reads=[PS(i)], writes=[("VT", tt, 1)])
        for tt in range(8):
            mms = []
            for dt in range(ND):
                mms.append(partial(
                    lambda t, tt=tt, dt=dt: t.matmul(
                        ps[:, 512 * 4 + 8 * tt:512 * 4 + 8 * (tt + 1)],
                        xnb[:, SEG * dt + 128 * tt:SEG * dt + 128 * (tt + 1)],
                        wrb[:, 8 * dt:8 * (dt + 1)],
                        start=(dt == 0), stop=(dt == ND - 1))))
            p.emit("tensor", mms,
                   reads=[("wrb",)] + [("xnb", dt) for dt in range(ND)],
                   writes=[PS(4)], accum=(tt > 0))
            p.emit("vector",
                   lambda v, tt=tt: v.reduce_max(
                       mxf[:, tt:tt + 1], ps[:, 512 * 4 + 8 * tt:512 * 4 + 8 * (tt + 1)], AX.X),
                   reads=[PS(4)], writes=[("mx", tt)])
            p.emit("vector",
                   lambda v, tt=tt: v.tensor_single_scalar(
                       mQf[:, 8 * tt:8 * (tt + 1)],
                       ps[:, 512 * 4 + 8 * tt:512 * 4 + 8 * (tt + 1)],
                       mxf[:, tt:tt + 1], ALU.is_equal),
                   reads=[PS(4), ("mx", tt)], writes=[("mQ", tt)])
            b0 = 5 + tt // 4
            co = 128 * (tt % 4)
            p.emit("tensor",
                   lambda t, tt=tt, b0=b0, co=co: t.transpose(
                       ps[0:8, 512 * b0 + co:512 * b0 + co + 128],
                       mQf[:, 8 * tt:8 * (tt + 1)], identf[:]),
                   reads=[("mQ", tt), ("identf",)], writes=[PS(b0)], accum=(tt % 4 > 0))
        for half in range(2):
            b0 = 5 + half
            p.emit("scalar",
                   lambda s, half=half, b0=b0: s.copy(
                       mTb[:, 512 * half:512 * (half + 1)], ps[0:8, 512 * b0:512 * (b0 + 1)]),
                   reads=[PS(b0)], writes=[("mT", half)])
            mms = []
            for q in range(4):
                mms.append(partial(
                    lambda t, half=half, q=q: t.matmul(
                        ps[:, 512 * 7 + 128 * q:512 * 7 + 128 * (q + 1)],
                        e16b[:], mTb[:, 512 * half + 128 * q:512 * half + 128 * (q + 1)],
                        start=True, stop=True)))
            p.emit("tensor", mms, reads=[("mT", half), ("e16b",)], writes=[PS(7)])
            p.emit("vector",
                   lambda v, half=half: v.tensor_copy(
                       mEb[:, 512 * half:512 * (half + 1)], bank(7)),
                   reads=[PS(7)], writes=[("mE", half)])
        for c in range(2):
            b0 = c
            mms = []
            for dt in range(ND):
                mms.append(partial(
                    lambda t, c=c, dt=dt, b0=b0: t.matmul(
                        bank(b0),
                        watb[:, 128 * dt:128 * (dt + 1)],
                        xnb[:, SEG * dt + 512 * c:SEG * dt + 512 * (c + 1)],
                        start=(dt == 0), stop=(dt == ND - 1))))
            p.emit("tensor", mms,
                   reads=[("watb",)] + [("xnb", dt) for dt in range(ND)],
                   writes=[PS(b0)])
            p.emit("scalar",
                   lambda s, c=c, b0=b0: s.copy(hb[:, 512 * c:512 * (c + 1)], bank(b0)),
                   reads=[PS(b0)], writes=[("h", c)])
            p.emit("vector",
                   lambda v, c=c: v.tensor_tensor(
                       hmb[:, 512 * c:512 * (c + 1)], hb[:, 512 * c:512 * (c + 1)],
                       mEb[:, 512 * c:512 * (c + 1)], ALU.mult),
                   reads=[("h", c), ("mE", c)], writes=[("hm", c)])
        proj(wq, 16, Q, "Q", 0.125)
        proj(wk, 4, K, "K", None)
        for ot in range(16):
            slot = 2 + ot % 2
            load_wblk(wb, ot, slot)
            for c in range(2):
                bG = 2 + (2 * ot + c) % 2
                mmg = []
                for dt in range(ND):
                    mmg.append(partial(
                        lambda t, dt=dt, c=c, slot=slot, bG=bG: t.matmul(
                            bank(bG),
                            ws[:, 2048 * slot + 128 * dt:2048 * slot + 128 * (dt + 1)],
                            xnb[:, SEG * dt + 512 * c:SEG * dt + 512 * (c + 1)],
                            start=(dt == 0), stop=False)))
                mmg.append(partial(
                    lambda t, ot=ot, c=c, bG=bG: t.matmul(
                        bank(bG),
                        wbctb[:, 128 * ot:128 * (ot + 1)],
                        hmb[:, 512 * c:512 * (c + 1)],
                        start=False, stop=True)))
                p.emit("tensor", mmg,
                       reads=[("ws", slot), ("wbctb",), ("hm", c)] + [("xnb", dt) for dt in range(ND)],
                       writes=[PS(bG)])
                p.emit("scalar",
                       lambda s, ot=ot, c=c, bG=bG: s.activation(
                           gb[:, SEG * ot + 512 * c:SEG * ot + 512 * (c + 1)], bank(bG),
                           AF.Silu, bias=bcol[:, ot:ot + 1]),
                       reads=[PS(bG), ("bcol",)], writes=[("g", ot, c)])

    def P3(u, qc, sc_banks, av_banks, pbd_bank, par0=None, esl_base=0, LA=3, nesl=4):
        nkt = 4 * qc + 4

        def make_normalize(blk, par, avE, avO):
            if par0 is not None:
                par = par0
            def n1():
                p.emit("scalar", [
                    partial(lambda s, par=par, avE=avE: s.copy(
                        denb[64:65, 512 * par:512 * (par + 1)], ps[64:65, 512 * avE:512 * avE + 512])),
                    partial(lambda s, par=par, avO=avO: s.copy(
                        denb[0:1, 512 * par:512 * (par + 1)], ps[0:1, 512 * avO:512 * avO + 512])),
                ], reads=[PS(avE), PS(avO)], writes=[("den", par)])

            def n2():
                p.emit("tensor", [
                    partial(lambda t, par=par: t.matmul(
                        bank(pbd_bank, 0, 64), onesr[64:65, 0:64],
                        denb[64:65, 512 * par:512 * (par + 1)], start=True, stop=True)),
                    partial(lambda t, par=par: t.matmul(
                        bank(pbd_bank, 64, 128), onesr[0:1, 0:64],
                        denb[0:1, 512 * par:512 * (par + 1)], start=True, stop=True)),
                ], reads=[("den", par), ("onesr",)], writes=[PS(pbd_bank)])

            def n3():
                p.emit("scalar",
                       lambda s, par=par: s.copy(pbds[:, 512 * par:512 * (par + 1)], bank(pbd_bank)),
                       reads=[PS(pbd_bank)], writes=[("pbds", par)])
                p.emit("vector",
                       lambda v, par=par: v.reciprocal(
                           pbr[:, 512 * par:512 * (par + 1)], pbds[:, 512 * par:512 * (par + 1)]),
                       reads=[("pbds", par)], writes=[("pbr", par)])

            def n4():
                p.emit("vector", [
                    partial(lambda v, blk=blk, par=par, qc=qc, avE=avE: v.tensor_tensor(
                        AOsl(qc, 0, 64, blk),
                        ps[0:64, 512 * avE:512 * avE + 512],
                        pbr[0:64, 512 * par:512 * (par + 1)], ALU.mult)),
                    partial(lambda v, blk=blk, par=par, qc=qc, avO=avO: v.tensor_tensor(
                        AOsl(qc, 64, 128, blk),
                        ps[64:128, 512 * avO:512 * avO + 512],
                        pbr[64:128, 512 * par:512 * (par + 1)], ALU.mult)),
                ], reads=[PS(avE), PS(avO), ("pbr", par)],
                   writes=[ao_key(qc, blk, 0), ao_key(qc, blk, 1), ("xnb", 8 * qc + blk // 2)])

            def n23():
                n2()
                n3()
            return [n1, n23, n4]

        def emit_scores(kp, blk, i, bE, bO):
            mmE = partial(
                lambda t, kp=kp, i=i, blk=blk, qc=qc, bE=bE: t.matmul(
                    bank(bE),
                    K[0:64, SEG * kp + 128 * i:SEG * kp + 128 * (i + 1)],
                    Q[0:64, SEG * blk + 512 * qc:SEG * blk + 512 * (qc + 1)],
                    start=True, stop=True))
            mmO = partial(
                lambda t, kp=kp, i=i, blk=blk, qc=qc, bO=bO: t.matmul(
                    bank(bO),
                    K[64:128, SEG * kp + 128 * i:SEG * kp + 128 * (i + 1)],
                    Q[64:128, SEG * blk + 512 * qc:SEG * blk + 512 * (qc + 1)],
                    start=True, stop=True))
            p.emit("tensor", [mmE, mmO],
                   reads=[("K", kp, i // 4), ("Q", blk, qc)],
                   writes=[PS(bE), PS(bO)])

        def emit_mask(esl, i, off):
            o = i - 4 * qc
            if o < 0:
                return
            p.emit("vector",
                   lambda v, esl=esl, off=off, o=o: v.tensor_tensor(
                       Eb[:, 1024 * esl + off:1024 * esl + off + 1024]
                       .rearrange("p (a c) -> p a c", a=2),
                       Eb[:, 1024 * esl + off:1024 * esl + off + 1024]
                       .rearrange("p (a c) -> p a c", a=2),
                       masksb[:, 512 * o:512 * (o + 1)]
                       .rearrange("p (a c) -> p a c", a=1).broadcast_to([128, 2, 512]),
                       ALU.mult),
                   reads=[("E", esl), ("masks",)], writes=[("E", esl)])

        def emit_av(kp, i, esl, off, first, last, avE, avO):
            avmE = partial(
                lambda t, kp=kp, i=i, esl=esl, off=off, first=first, last=last, avE=avE: t.matmul(
                    bank(avE),
                    VT[:, 768 * i + 192 * kp:768 * i + 192 * kp + 128],
                    Eb[:, 1024 * esl + off:1024 * esl + off + 512],
                    start=first, stop=last))
            avmO = partial(
                lambda t, kp=kp, i=i, esl=esl, off=off, first=first, last=last, avO=avO: t.matmul(
                    bank(avO),
                    VT[:, 768 * i + 192 * kp + 64:768 * i + 192 * kp + 192],
                    Eb[:, 1024 * esl + off + 512:1024 * esl + off + 1024],
                    start=first, stop=last))
            p.emit("tensor", [avmE, avmO],
                   reads=[("VT", i, 0), ("VT", i, 1), ("vtc",), ("E", esl)],
                   writes=[PS(avE), PS(avO)], accum=(not first))

        pending = []
        for kp in range(4):
            for j in range(4):
                blk = 4 * kp + j
                avE, avO = av_banks[(4 * kp + j) % len(av_banks)]
                for step in range(nkt + LA):
                    if step < nkt:
                        i = step
                        bE = sc_banks[2 * (i % 2)]
                        bO = sc_banks[2 * (i % 2) + 1]
                        esl = esl_base + i % nesl
                        emit_scores(kp, blk, i, bE, bO)
                        p.emit("scalar",
                               lambda s, esl=esl, bE=bE: s.activation(
                                   Eb[:, 1024 * esl:1024 * (esl + 1)],
                                   ps[:, 512 * bE:512 * bE + 1024], AF.Exp),
                               reads=[PS(bE), PS(bE + 1)], writes=[("E", esl)])
                        emit_mask(esl, i, 0)
                    if step >= LA:
                        i = step - LA
                        esl = esl_base + i % nesl
                        emit_av(kp, i, esl, 0, i == 0, i == nkt - 1, avE, avO)
                    if step >= 1 and step % 2 == 1 and pending:
                        pending.pop(0)()
                    yield
                if len(av_banks) > 1:
                    while pending:
                        pending.pop(0)()
                    pending = list(make_normalize(blk, j % 2, avE, avO))
                else:
                    for fn in make_normalize(blk, j % 2, avE, avO):
                        fn()
                    yield
        while pending:
            pending.pop(0)()
            yield

    def P4(u, qc, sbanks, wslots):
        tc0 = u * SEG
        ao_reads = [ao_key(qc, dtb, hf) for dtb in range(16) for hf in range(2)]
        for ot in range(16):
            slot = wslots[ot % len(wslots)]
            load_wblk(wo, ot, slot)
            bS = sbanks[ot % 2]
            p.emit("scalar",
                   lambda s, ot=ot, qc=qc, slot=ot % 2, tc0=tc0: s.dma_start(
                       out=xre[:, 512 * slot:512 * (slot + 1)],
                       in_=xdt[128 * ot:128 * (ot + 1), tc0 + 512 * qc:tc0 + 512 * (qc + 1)]),
                   writes=[("xre", ot % 2)], chain="xre%d" % (ot % 2))
            for piece in range(8):
                mms = []
                for dt in range(2 * piece, 2 * piece + 2):
                    mms.append(partial(
                        lambda t, dt=dt, slot=slot, bS=bS, qc=qc: t.matmul(
                            bank(bS),
                            ws[:, 2048 * slot + 128 * dt:2048 * slot + 128 * (dt + 1)],
                            AOsl(qc, 0, 128, dt),
                            start=(dt == 0), stop=(dt == ND - 1))))
                p.emit("tensor", mms,
                       reads=[("ws", slot)] + ao_reads,
                       writes=[PS(bS)], accum=(piece > 0))
                yield
            p.emit("vector",
                   lambda v, ot=ot, qc=qc, slot=ot % 2, bS=bS: v.tensor_tensor(
                       sgf[:, 512 * slot:512 * (slot + 1)], bank(bS),
                       gb[:, SEG * ot + 512 * qc:SEG * ot + 512 * (qc + 1)], ALU.mult),
                   reads=[PS(bS), ("g", ot, qc)], writes=[("sg", ot % 2)])
            p.emit("vector",
                   lambda v, slot=ot % 2: v.tensor_tensor(
                       stg[:, 512 * slot:512 * (slot + 1)],
                       sgf[:, 512 * slot:512 * (slot + 1)],
                       xre[:, 512 * slot:512 * (slot + 1)], ALU.add),
                   reads=[("sg", ot % 2), ("xre", ot % 2)], writes=[("stg", ot % 2)])
            p.emit("sync",
                   lambda s, ot=ot, qc=qc, slot=ot % 2, tc0=tc0: s.dma_start(
                       out=outT[128 * ot:128 * (ot + 1), tc0 + 512 * qc:tc0 + 512 * (qc + 1)],
                       in_=stg[:, 512 * slot:512 * (slot + 1)]),
                   reads=[("stg", ot % 2)], chain="st%d" % (ot % 2))
            yield

    prevP4 = None
    for u in units:
        if prevP4 is not None:
            drive([(prevP4, 2), (P1a(u), 1)])
        else:
            drive([(P1a(u), 1)])
        P1b(u)
        P2(u)
        drive([(P3(u, 0, (0, 1, 2, 3), [(4, 5), (6, 7)], 0), 1)])
        drive([(P3(u, 1, (0, 1, 2, 3), [(4, 5)], 0), 1),
               (P4(u, 0, (6, 7), (0, 1, 2, 3)), 1)])
        prevP4 = P4(u, 1, (0, 1), (0, 1, 2, 3))
    if prevP4 is not None:
        drive([(prevP4, 1)])

    for sl in range(2):
        name = "st%d" % sl
        if name in p.dchain:
            p.raw_wait("sync", ("d", name), 16 * p.dchain[name])

    import os
    if os.environ.get("KPROBE_SBUF"):
        print("SBUF remaining B/part:", nc.sbuf_bytes_remaining)
    p.replay()
    ctx.close()
    return nc


def _host_prep(inputs):
    f32 = np.float32
    x = np.asarray(inputs["L0_attn_hidden_states"], dtype=f32)
    Wq = np.asarray(inputs["L0_attn_Wq"], dtype=f32)
    Wk = np.asarray(inputs["L0_attn_Wk"], dtype=f32)
    Wv = np.asarray(inputs["L0_attn_Wv"], dtype=f32)
    Wo = np.asarray(inputs["L0_attn_Wo"], dtype=f32)
    Wb = np.asarray(inputs["L0_ffn_W_base"], dtype=f32)
    bb = np.asarray(inputs["L0_ffn_b_base"], dtype=f32)
    rW = np.asarray(inputs["L0_ffn_router_W"], dtype=f32)
    A = np.asarray(inputs["L0_ffn_A"], dtype=f32)
    Bw = np.asarray(inputs["L0_ffn_Bw"], dtype=f32)

    qperm = np.empty(D, dtype=np.int64)
    for blk in range(16):
        for s in range(2):
            h = 8 * (blk // 4) + 4 * s + (blk % 4)
            qperm[128 * blk + 64 * s: 128 * blk + 64 * s + 64] = np.arange(64) + 64 * h

    def blocks(W):
        return np.ascontiguousarray(
            W.reshape(16, 128, 16, 128).transpose(0, 3, 2, 1).reshape(16, 128, 2048))

    wq_h = blocks(Wq[qperm, :])
    wo_h = blocks(Wo[:, qperm])
    wb_h = blocks(Wb)
    wk_h = np.ascontiguousarray(Wk.reshape(4, 128, 16, 128).transpose(0, 3, 2, 1).reshape(4, 128, 2048))
    wv_h = np.ascontiguousarray(Wv.T.reshape(4, 4, 128, 512).transpose(0, 2, 1, 3).reshape(4, 128, 2048))
    wr_h = np.ascontiguousarray(rW.T.reshape(16, 128, 8).transpose(1, 0, 2).reshape(128, 128))
    wat_h = np.ascontiguousarray(
        A.reshape(128, 2048).T.reshape(16, 128, 128).transpose(1, 0, 2).reshape(128, 2048))
    wbct_h = np.ascontiguousarray(Bw.transpose(0, 2, 1).reshape(128, 2048))
    bb_h = np.ascontiguousarray(bb.reshape(16, 128))
    e16_h = np.zeros((8, 128), dtype=f32)
    for ex in range(8):
        e16_h[ex, 16 * ex:16 * (ex + 1)] = 2.0
    ident_h = np.eye(128, dtype=f32)
    masks_h = np.zeros((128, 2048), dtype=f32)
    for o in range(4):
        masks_h[:, 512 * o:512 * (o + 1)] = (
            np.arange(512)[None, :] >= (128 * o + np.arange(128))[:, None]).astype(f32)
    x_dt = np.ascontiguousarray(x.transpose(0, 2, 1))

    shared = {
        "wq": wq_h, "wk": wk_h, "wv": wv_h, "wo": wo_h, "wb": wb_h,
        "wr": wr_h, "wat": wat_h, "wbct": wbct_h, "bb": bb_h,
        "e16": e16_h, "ident": ident_h, "masks": masks_h,
    }
    return x_dt, shared


def kernel(**inputs):
    x_dt, shared = _host_prep(inputs)
    nc = build_nc()
    in_maps = []
    for c in range(B):
        m = dict(shared)
        m["x_dt"] = np.ascontiguousarray(x_dt[c])
        in_maps.append(m)
    res = run_bass_kernel_spmd(nc, in_maps, core_ids=list(range(B)))
    out = np.stack([np.asarray(r["out_T"]) for r in res.results])
    return np.ascontiguousarray(out.transpose(0, 2, 1)).astype(np.float32)
